# version 02
# baseline (speedup 1.0000x reference)
import numpy as np
import ml_dtypes
from contextlib import ExitStack

import concourse.bass as bass
import concourse.mybir as mybir
from concourse.bass_utils import run_bass_kernel_spmd

F32 = mybir.dt.float32
BF16 = mybir.dt.bfloat16
U32 = mybir.dt.uint32
U16 = mybir.dt.uint16
I32 = mybir.dt.int32
ALU = mybir.AluOpType
AF = mybir.ActivationFunctionType
AX = mybir.AxisListType
bfdt = ml_dtypes.bfloat16

NCORES = 8
DM = 4096
TOK = 1024
NTT = 8
INTER = 11008
GU = 2 * INTER
NPAIR = 86
NRB1 = 4
NRB2 = 11
NST = 64
KPAD0 = 2048
KHBM_LEN = KPAD0 + DM
SKEW_W = 5119
EPS = 1e-5
TPI = float(2.0 * np.pi / 4096.0)


def _host_constants():
    c = {}
    c["identf"] = np.eye(128, dtype=np.float32)
    c["onesf"] = np.ones((1, 128), dtype=np.float32)
    k = np.arange(DM).reshape(32, 128).T.astype(np.float32)
    c["iotaK"] = np.ascontiguousarray(np.where(k >= 2048, k - 4096.0, k).astype(np.float32))
    c["iota128c"] = np.arange(128, dtype=np.float32).reshape(128, 1)
    c["iota32b"] = np.tile(np.arange(32, dtype=np.float32)[None, :], (128, 1))
    k1 = np.arange(NST, dtype=np.float64)
    ang1 = 2.0 * np.pi * np.outer(k1, k1) / NST
    c["w64c"] = np.cos(ang1).astype(np.float32)
    c["w64s"] = np.sin(ang1).astype(np.float32)
    c["w64sn"] = (-np.sin(ang1)).astype(np.float32)
    angt = 2.0 * np.pi * np.outer(k1, k1) / float(DM)
    c["twc"] = (np.cos(angt) / float(DM)).astype(np.float32)
    c["tws"] = (np.sin(angt) / float(DM)).astype(np.float32)
    return c


class Sched:

    def __init__(self, nc, ctx):
        self.nc = nc
        self.steps = []
        self.totals = {"T": 0, "V": 0, "S": 0, "Y": 0, "SD": 0}
        self.emitted = 0
        self.sems = {c: ctx.enter_context(nc.semaphore("sem" + c)) for c in self.totals}

    def add(self, cls, fn, deps="prev"):
        i = len(self.steps)
        if deps == "prev":
            deps = [i - 1] if i > 0 else []
        deps = sorted({d for d in deps if d is not None and d >= 0})
        inc = 16 if cls in ("Y", "SD") else 1
        self.totals[cls] += inc
        self.steps.append((cls, fn, deps, self.totals[cls], inc))
        return i

    def last_v(self):
        for i in range(len(self.steps) - 1, -1, -1):
            if self.steps[i][0] == "V":
                return i
        return None

    def run_block(self):
        lo, hi = self.emitted, len(self.steps)
        self.emitted = hi
        start_tot = {c: 0 for c in self.totals}
        for i in range(lo):
            c = self.steps[i][0]
            start_tot[c] = self.steps[i][3]

        def emit(classes, eng):
            for c, tot in start_tot.items():
                if c not in classes and tot > 0:
                    eng.wait_ge(self.sems[c], tot)
            for i in range(lo, hi):
                cls, fn, deps, after, inc = self.steps[i]
                if cls not in classes:
                    continue
                waits = {}
                for d in deps:
                    dc = self.steps[d][0]
                    if dc == cls:
                        continue
                    waits[dc] = max(waits.get(dc, 0), self.steps[d][3])
                for dc, val in waits.items():
                    eng.wait_ge(self.sems[dc], val)
                last = fn(eng)
                last.then_inc(self.sems[cls], inc)

        with self.nc.Block() as blk:
            @blk.sync
            def _(eng):
                emit(("Y",), eng)

            @blk.scalar
            def _(eng):
                emit(("S", "SD"), eng)

            @blk.vector
            def _(eng):
                emit(("V",), eng)

            @blk.tensor
            def _(eng):
                emit(("T",), eng)


DEBUG = False


def build_program(debug=False):
    nc = bass.Bass("TRN2", target_bir_lowering=False)
    DT = lambda name, shp, dt, kind: nc.dram_tensor(name, shp, dt, kind=kind).ap()
    x_in = DT("x", [TOK, DM], F32, "ExternalInput")
    lam = DT("lam", [NST, 2], F32, "ExternalInput")
    pv = DT("pv", [NST, 2], F32, "ExternalInput")
    qv = DT("qv", [NST, 2], F32, "ExternalInput")
    vc = DT("vc", [NST, 2 * NST], F32, "ExternalInput")
    ct = DT("ct", [NST, 2], F32, "ExternalInput")
    bp = DT("bp", [NST, 1], F32, "ExternalInput")
    dd = DT("dd", [1, 1], F32, "ExternalInput")
    ls = DT("ls", [1, 1], F32, "ExternalInput")
    qw1 = DT("qw1", [512, GU], U32, "ExternalInput")
    qz1 = DT("qz1", [32, 2752], U32, "ExternalInput")
    sc1 = DT("sc1", [32, GU], F32, "ExternalInput")
    qw2 = DT("qw2", [1376, DM], U32, "ExternalInput")
    qz2 = DT("qz2", [86, 512], U32, "ExternalInput")
    sc2 = DT("sc2", [86, DM], F32, "ExternalInput")
    g1 = DT("g1", [DM], I32, "ExternalInput")
    g2 = DT("g2", [INTER], I32, "ExternalInput")
    identf = DT("identf", [128, 128], F32, "ExternalInput")
    onesf = DT("onesf", [1, 128], F32, "ExternalInput")
    iotaK = DT("iotaK", [128, 32], F32, "ExternalInput")
    iota128c = DT("iota128c", [128, 1], F32, "ExternalInput")
    iota32b = DT("iota32b", [128, 32], F32, "ExternalInput")
    w64c = DT("w64c", [NST, NST], F32, "ExternalInput")
    w64s = DT("w64s", [NST, NST], F32, "ExternalInput")
    w64sn = DT("w64sn", [NST, NST], F32, "ExternalInput")
    twc = DT("twc", [NST, NST], F32, "ExternalInput")
    tws = DT("tws", [NST, NST], F32, "ExternalInput")
    y_out = DT("y", [TOK, DM], F32, "ExternalOutput")
    if debug:
        dbg_xnT = DT("dbg_xnT", [128, 32 * TOK], BF16, "ExternalOutput")
        dbg_S1b = DT("dbg_S1b", [32, TOK], BF16, "ExternalOutput")
        dbg_Z1b = DT("dbg_Z1b", [32, GU], BF16, "ExternalOutput")
        dbg_ln = DT("dbg_ln", [128, 64], F32, "ExternalOutput")
        dbg_kg = DT("dbg_kg", [128, 768], F32, "ExternalOutput")
    SCRK = "ExternalOutput" if debug else "Internal"
    khbm = DT("khbm", [KHBM_LEN], BF16, SCRK)
    ar_hbm = DT("ar_hbm", [2, DM], F32, SCRK)
    vrows_hbm = DT("vrows_hbm", [12, NST], F32, "Internal")
    grow1_hbm = DT("grow1_hbm", [NRB1, 128], F32, "Internal")
    grow2_hbm = DT("grow2_hbm", [NRB2, 128], F32, "Internal")
    s_s = DT("s_s", [TOK, DM], BF16, SCRK)
    hact_s = DT("hact_s", [INTER, TOK], BF16, SCRK)
    g_s = DT("g_s", [DM, TOK], BF16, SCRK)
    sc1e_s = DT("sc1e_s", [NRB1 * 128, GU], F32, SCRK)
    sc2e_s = DT("sc2e_s", [NRB2 * 128, DM], F32, SCRK)
    z2e_s = DT("z2e_s", [NRB2 * 128, DM], F32, SCRK)

    with ExitStack() as ctx:
        E = ctx.enter_context
        SB = lambda name, shp, dt: E(nc.sbuf_tensor(name, shp, dt))
        idf = SB("idf", [128, 128], F32)
        onf = SB("onf", [1, 128], F32)
        iotK = SB("iotK", [128, 32], F32)
        iot1 = SB("iot1", [128, 1], F32)
        io32b = SB("io32b", [128, 32], F32)
        xnT = SB("xnT", [128, 32 * TOK], BF16)
        Z1b = SB("Z1b", [32, GU], BF16)
        S1b = SB("S1b", [32, TOK], BF16)
        OH1 = SB("OH1", [128, 32 * 32], BF16)
        kpsk = SB("kpsk", [128, SKEW_W], BF16)
        esb = SB("esb", [128, DM], BF16)
        PS = E(nc.psum_tensor("PS", [128, 4096], F32))
        BANK = lambda b: PS[:, 512 * b:512 * (b + 1)]

        sch = Sched(nc, ctx)
        A = sch.add

        base_loads = []
        for dst, src in [(idf[:], identf), (onf[:], onesf), (iotK[:], iotaK),
                         (iot1[:], iota128c), (io32b[:], iota32b)]:
            base_loads.append(A("Y", (lambda d, s: lambda e: e.dma_start(d, s))(dst, src), deps=[]))

        with ExitStack() as c1:
            SBP = lambda name, shp, dt: c1.enter_context(nc.sbuf_tensor(name, shp, dt))
            vct = SBP("vct", [NST, 2 * NST], F32)
            vcT = SBP("vcT", [NST, 2 * NST], F32)
            bpt = SBP("bpt", [NST, 1], F32)
            ctt = SBP("ctt", [NST, 2], F32)
            lamt = SBP("lamt", [NST, 2], F32)
            pvt = SBP("pvt", [NST, 2], F32)
            qvt = SBP("qvt", [NST, 2], F32)
            sca = SBP("sca", [1, 8], F32)
            small = SBP("small", [NST, 16], F32)
            t2s = SBP("t2s", [128, 1], F32)
            vb = SBP("vb", [128, 10 * NST], F32)
            w64ct = SBP("w64ct", [NST, NST], F32)
            w64st = SBP("w64st", [NST, NST], F32)
            w64snt = SBP("w64snt", [NST, NST], F32)
            twct = SBP("twct", [NST, NST], F32)
            twst = SBP("twst", [NST, NST], F32)
            arsb = SBP("arsb", [128, 64], F32)
            arTT = SBP("arTT", [64, 128], F32)
            lhs_re = SBP("lhs_re", [NST, NST], F32)
            lhs_im = SBP("lhs_im", [NST, NST], F32)
            Ap_re = SBP("Ap_re", [NST, NST], F32)
            Ap_im = SBP("Ap_im", [NST, NST], F32)
            Ksb = SBP("Ksb", [NST, NST], F32)
            KTb = SBP("KTb", [NST, NST], BF16)
            zrow = SBP("zrow", [1, KPAD0], BF16)
            ktc = SBP("ktc", [128, 44], F32)
            wide = SBP("wide", [128, 4 * NST], F32)
            vrow = SBP("vrow", [1, NST], F32)
            kb = SBP("kb", [128, 32 * 44], F32)
            B1 = SBP("B1", [128, 2048], F32)
            B2 = SBP("B2", [128, 2048], F32)
            B3 = SBP("B3", [128, 2048], F32)
            B4 = SBP("B4", [128, 2048], F32)
            B5 = SBP("B5", [128, 2048], F32)
            B6 = SBP("B6", [128, 2048], F32)
            BT64 = SBP("BT64", [128, 128], F32)
            zro64 = SBP("zro64", [128, NST], F32)
            dbgk = SBP("dbgk", [128, 768], F32) if debug else None

            lds = list(base_loads)
            for dst, src in [(vct[:], vc), (bpt[:], bp), (ctt[:], ct), (lamt[:], lam),
                             (pvt[:], pv), (qvt[:], qv), (sca[:, 0:1], ls), (sca[:, 1:2], dd),
                             (w64ct[:], w64c), (w64st[:], w64s), (w64snt[:], w64sn),
                             (twct[:], twc), (twst[:], tws)]:
                lds.append(A("Y", (lambda d, s: lambda e: e.dma_start(d, s))(dst, src), deps=[]))
            A("S", lambda e: e.activation(sca[:, 2:3], sca[:, 0:1], AF.Exp), deps=lds)
            A("V", lambda e: (e.reciprocal(sca[:, 3:4], sca[:, 2:3]), e.drain(),
                              e.tensor_scalar(sca[:, 4:5], sca[:, 3:4], 2.0, None, ALU.mult))[-1])
            A("T", lambda e: (
                e.transpose(PS[0:NST, 2048:2048 + NST], bass.AP(vct, 0, [[2 * NST, NST], [2, NST]]), idf[0:NST, 0:NST]),
                e.transpose(PS[0:NST, 2176:2176 + NST], bass.AP(vct, 1, [[2 * NST, NST], [2, NST]]), idf[0:NST, 0:NST]))[-1])
            A("V", lambda e: (
                e.tensor_copy(vcT[:, 0:NST], PS[0:NST, 2048:2048 + NST]),
                e.tensor_copy(vcT[:, NST:2 * NST], PS[0:NST, 2176:2176 + NST]))[-1])
            A("T", lambda e: (
                e.matmul(PS[0:NST, 2048:2049], vcT[:, 0:NST], bpt[:], start=True, stop=True),
                e.matmul(PS[0:NST, 2176:2177], vcT[:, NST:2 * NST], bpt[:], start=True, stop=True),
                e.matmul(PS[0:128, 2304:2305], onf[:], sca[0:1, 4:5], start=True, stop=True))[-1])

            def _vsmall(e):
                e.tensor_copy(small[:, 0:1], PS[0:NST, 2048:2049])
                e.tensor_copy(small[:, 1:2], PS[0:NST, 2176:2177])
                e.tensor_copy(t2s[:], PS[0:128, 2304:2305])
                e.drain()
                ctr, cti = ctt[:, 0:1], ctt[:, 1:2]
                qr, qi = qvt[:, 0:1], qvt[:, 1:2]
                pr, pi = pvt[:, 0:1], pvt[:, 1:2]
                br, bi = small[:, 0:1], small[:, 1:2]
                t0, t1 = small[:, 10:11], small[:, 11:12]

                def cmc(outr, outi, a_r, a_i, b_r, b_i):
                    e.tensor_tensor(t0, a_r, b_r, ALU.mult)
                    e.tensor_tensor(t1, a_i, b_i, ALU.mult)
                    e.drain()
                    e.tensor_tensor(outr, t0, t1, ALU.add)
                    e.drain()
                    e.tensor_tensor(t0, a_r, b_i, ALU.mult)
                    e.drain()
                    e.tensor_tensor(t1, a_i, b_r, ALU.mult)
                    e.drain()
                    e.tensor_tensor(outi, t0, t1, ALU.subtract)
                    e.drain()
                cmc(small[:, 2:3], small[:, 3:4], ctr, cti, br, bi)
                cmc(small[:, 4:5], small[:, 5:6], ctr, cti, pr, pi)
                cmc(small[:, 6:7], small[:, 7:8], qr, qi, br, bi)
                cmc(small[:, 8:9], small[:, 9:10], qr, qi, pr, pi)
                e.tensor_copy(small[:, 12:13], lamt[:, 0:1])
                return e.tensor_copy(small[:, 13:14], lamt[:, 1:2])
            A("V", _vsmall)
            A("T", lambda e: e.transpose(PS[0:12, 2048:2048 + NST],
                                         bass.AP(small, 2, [[16, NST], [1, 12]]), idf[0:NST, 0:NST]))
            A("V", lambda e: e.tensor_copy(wide[0:12, 0:NST], PS[0:12, 2048:2048 + NST]))
            A("Y", lambda e: e.dma_start(vrows_hbm[0:12, :], wide[0:12, 0:NST]))
            for r, hrow in enumerate([0, 1, 2, 3, 4, 5, 6, 7, 10, 11]):
                A("Y", (lambda hr: lambda e: e.dma_start(vrow[:], vrows_hbm[hr:hr + 1, :]))(hrow))
                A("T", lambda e: e.matmul(PS[0:128, 2048:2048 + NST], onf[:],
                                          vrow[0:1, :], start=True, stop=True))
                A("V", (lambda r: lambda e: e.tensor_copy(
                    vb[:, r * NST:(r + 1) * NST], PS[0:128, 2048:2048 + NST]))(r))
            if debug:
                A("V", lambda e: (e.tensor_copy(dbgk[:, 0:640], vb[:]),
                                  e.tensor_copy(dbgk[:, 640:641], t2s[:]))[-1])
            KB = lambda i: kb[:, 32 * i:32 * (i + 1)]
            A("V", lambda e: (e.tensor_scalar(KB(0), iotK[:], TPI, None, ALU.mult),
                              e.drain(), e.memset(zro64[:], 0))[-1])
            A("S", lambda e: (e.activation(KB(1), KB(0), AF.Sin),
                              e.activation(KB(2), KB(0), AF.Sin, scale=0.5))[-1])

            def _vka(e):
                d = e.drain
                e.tensor_tensor(KB(3), KB(2), KB(2), ALU.mult); d()
                e.tensor_scalar(KB(3), KB(3), -2.0, 1.0, ALU.mult, ALU.add); d()
                e.tensor_scalar(KB(4), KB(3), 1.0, None, ALU.add)
                e.tensor_scalar(KB(5), KB(3), -1.0, 1.0, ALU.mult, ALU.add)
                d()
                e.tensor_tensor(KB(41), KB(4), KB(4), ALU.mult)
                e.tensor_tensor(KB(40), KB(1), KB(1), ALU.mult)
                d()
                e.tensor_tensor(KB(41), KB(41), KB(40), ALU.add); d()
                e.reciprocal(KB(6), KB(41)); d()
                e.tensor_tensor(KB(42), KB(5), KB(4), ALU.mult); d()
                e.tensor_tensor(KB(42), KB(42), KB(40), ALU.subtract); d()
                e.tensor_tensor(KB(42), KB(42), KB(6), ALU.mult); d()
                e.tensor_scalar(KB(7), KB(42), t2s[:], None, ALU.mult); d()
                e.tensor_tensor(KB(42), KB(1), KB(4), ALU.mult)
                e.tensor_tensor(KB(43), KB(1), KB(5), ALU.mult)
                d()
                e.tensor_tensor(KB(42), KB(42), KB(43), ALU.add); d()
                e.tensor_tensor(KB(42), KB(42), KB(6), ALU.mult); d()
                e.tensor_scalar(KB(8), KB(42), t2s[:], None, ALU.mult); d()
                e.tensor_scalar(KB(42), KB(4), 2.0, None, ALU.mult)
                e.tensor_scalar(KB(43), KB(1), 2.0, None, ALU.mult)
                d()
                e.tensor_tensor(KB(9), KB(42), KB(6), ALU.mult)
                e.tensor_tensor(KB(10), KB(43), KB(6), ALU.mult)
                d()
                e.tensor_copy(B1[:, 0:64], vb[:, 8 * NST:9 * NST])
                e.tensor_copy(B2[:, 0:64], vb[:, 9 * NST:10 * NST])
                d()
                wdt = 64
                while wdt < 2048:
                    e.tensor_copy(B1[:, wdt:2 * wdt], B1[:, 0:wdt])
                    e.tensor_copy(B2[:, wdt:2 * wdt], B2[:, 0:wdt])
                    d()
                    wdt *= 2
                for kk in range(32):
                    e.tensor_scalar(B3[:, 64 * kk:64 * (kk + 1)], zro64[:],
                                    KB(7)[:, kk:kk + 1], None, ALU.add)
                    e.tensor_scalar(B4[:, 64 * kk:64 * (kk + 1)], zro64[:],
                                    KB(8)[:, kk:kk + 1], None, ALU.add)
                d()
                e.tensor_tensor(B5[:], B1[:], B3[:], ALU.subtract)
                e.tensor_tensor(B6[:], B2[:], B4[:], ALU.subtract)
                d()
                e.tensor_tensor(B3[:], B5[:], B5[:], ALU.mult)
                e.tensor_tensor(B4[:], B6[:], B6[:], ALU.mult)
                d()
                e.tensor_tensor(B3[:], B3[:], B4[:], ALU.add); d()
                e.reciprocal(B4[:], B3[:]); d()
                e.tensor_tensor(B5[:], B5[:], B4[:], ALU.mult)
                e.tensor_tensor(B6[:], B6[:], B4[:], ALU.mult)
                d()
                for pi in range(4):
                    vr = vb[:, (2 * pi) * NST:(2 * pi + 1) * NST]
                    vi = vb[:, (2 * pi + 1) * NST:(2 * pi + 2) * NST]
                    for half in range(2):
                        e.tensor_copy(B3[:, 0:64], vr)
                        e.tensor_copy(B4[:, 0:64], vi)
                        d()
                        wdt = 64
                        while wdt < 2048:
                            e.tensor_copy(B3[:, wdt:2 * wdt], B3[:, 0:wdt])
                            e.tensor_copy(B4[:, wdt:2 * wdt], B4[:, 0:wdt])
                            d()
                            wdt *= 2
                        if half == 0:
                            e.tensor_tensor(B3[:], B3[:], B5[:], ALU.mult)
                            e.tensor_tensor(B4[:], B4[:], B6[:], ALU.mult)
                        else:
                            e.tensor_tensor(B3[:], B3[:], B6[:], ALU.mult)
                            e.tensor_tensor(B4[:], B4[:], B5[:], ALU.mult)
                        d()
                        v3 = B3[:].rearrange("p (t n) -> p t n", t=32, n=64)
                        v4 = B4[:].rearrange("p (t n) -> p t n", t=32, n=64)
                        e.tensor_reduce(KB(11 + 4 * pi + 2 * half), v3, AX.X, ALU.add)
                        e.tensor_reduce(KB(12 + 4 * pi + 2 * half), v4, AX.X, ALU.add)
                        d()
                d()
                for pi in range(4):
                    e.tensor_tensor(KB(27 + 2 * pi), KB(11 + 4 * pi), KB(12 + 4 * pi), ALU.add)
                    e.tensor_tensor(KB(35 + 0) if False else KB(28 + 2 * pi),
                                    KB(13 + 4 * pi), KB(14 + 4 * pi), ALU.subtract)
                d()
                for pi in range(4):
                    e.tensor_scalar(KB(27 + 2 * pi), KB(27 + 2 * pi), -1.0, None, ALU.mult)
                d()
                k00r, k00i = KB(27), KB(28)
                k01r, k01i = KB(29), KB(30)
                k10r, k10i = KB(31), KB(32)
                k11r, k11i = KB(33), KB(34)
                wr, wi = KB(35), KB(36)
                t1, t2, t3, t4 = KB(37), KB(38), KB(39), KB(40)
                e.tensor_scalar(wr, k11r, 1.0, None, ALU.add); d()
                e.tensor_tensor(t1, wr, wr, ALU.mult)
                e.tensor_tensor(t2, k11i, k11i, ALU.mult)
                d()
                e.tensor_tensor(t1, t1, t2, ALU.add); d()
                e.reciprocal(t2, t1); d()
                e.tensor_tensor(t3, k01r, wr, ALU.mult)
                e.tensor_tensor(t4, k01i, k11i, ALU.mult)
                d()
                e.tensor_tensor(t3, t3, t4, ALU.add); d()
                e.tensor_tensor(t3, t3, t2, ALU.mult); d()
                e.tensor_tensor(t4, k01i, wr, ALU.mult)
                e.tensor_tensor(t1, k01r, k11i, ALU.mult)
                d()
                e.tensor_tensor(t4, t4, t1, ALU.subtract); d()
                e.tensor_tensor(t4, t4, t2, ALU.mult); d()
                e.tensor_tensor(t1, t3, k10r, ALU.mult)
                e.tensor_tensor(t2, t4, k10i, ALU.mult)
                d()
                e.tensor_tensor(t1, t1, t2, ALU.subtract); d()
                e.tensor_tensor(t2, t3, k10i, ALU.mult); d()
                e.tensor_tensor(t3, t4, k10r, ALU.mult); d()
                e.tensor_tensor(t2, t2, t3, ALU.add); d()
                e.tensor_tensor(t1, k00r, t1, ALU.subtract)
                e.tensor_tensor(t2, k00i, t2, ALU.subtract)
                d()
                e.tensor_tensor(t3, KB(9), t1, ALU.mult)
                e.tensor_tensor(t4, KB(10), t2, ALU.mult)
                d()
                e.tensor_tensor(arsb[:, 0:32], t3, t4, ALU.subtract); d()
                e.tensor_tensor(t3, KB(9), t2, ALU.mult)
                e.tensor_tensor(t4, KB(10), t1, ALU.mult)
                d()
                e.tensor_tensor(arsb[:, 32:64], t3, t4, ALU.add)
                return e.drain()
            A("V", _vka)
            if debug:
                A("V", lambda e: (e.tensor_copy(dbgk[:, 644:645], KB(7)[:, 0:1]),
                                  e.tensor_copy(dbgk[:, 645:646], KB(8)[:, 0:1]),
                                  e.tensor_copy(dbgk[:, 646:647], KB(3)[:, 0:1]),
                                  e.tensor_copy(dbgk[:, 647:648], KB(27)[:, 0:1]),
                                  e.tensor_copy(dbgk[:, 648:649], KB(28)[:, 0:1]),
                                  e.tensor_copy(dbgk[:, 688:752], arsb[:]))[-1])
            A("T", lambda e: e.transpose(PS[0:64, 2048:2176], arsb[:], idf[:]))
            A("V", lambda e: e.tensor_copy(arTT[:], PS[0:64, 2048:2176]))
            A("Y", lambda e: e.dma_start(
                bass.AP(ar_hbm.tensor, 0, [[128, 32], [1, 128]]), arTT[0:32, :]))
            A("Y", lambda e: e.dma_start(
                bass.AP(ar_hbm.tensor, DM, [[128, 32], [1, 128]]), arTT[32:64, :]))
            A("Y", lambda e: e.dma_start(lhs_re[:], bass.AP(ar_hbm.tensor, 0, [[NST, NST], [1, NST]])))
            A("Y", lambda e: e.dma_start(lhs_im[:], bass.AP(ar_hbm.tensor, DM, [[NST, NST], [1, NST]])))
            A("T", lambda e: (
                e.matmul(PS[0:NST, 2048:2048 + NST], lhs_re[:], w64ct[:], start=True, stop=False),
                e.matmul(PS[0:NST, 2048:2048 + NST], lhs_im[:], w64snt[:], start=False, stop=True),
                e.matmul(PS[0:NST, 2176:2176 + NST], lhs_re[:], w64st[:], start=True, stop=False),
                e.matmul(PS[0:NST, 2176:2176 + NST], lhs_im[:], w64ct[:], start=False, stop=True))[-1])

            def _vtw(e):
                e.tensor_copy(lhs_re[:], PS[0:NST, 2048:2048 + NST])
                e.tensor_copy(lhs_im[:], PS[0:NST, 2176:2176 + NST])
                e.drain()
                e.tensor_tensor(Ap_re[:], lhs_re[:], twct[:], ALU.mult)
                e.tensor_tensor(wide[0:NST, 0:NST], lhs_im[:], twst[:], ALU.mult)
                e.drain()
                e.tensor_tensor(Ap_re[:], Ap_re[:], wide[0:NST, 0:NST], ALU.subtract)
                e.tensor_tensor(Ap_im[:], lhs_re[:], twst[:], ALU.mult)
                e.drain()
                e.tensor_tensor(wide[0:NST, 0:NST], lhs_im[:], twct[:], ALU.mult)
                e.drain()
                return e.tensor_tensor(Ap_im[:], Ap_im[:], wide[0:NST, 0:NST], ALU.add)
            A("V", _vtw)
            A("T", lambda e: (
                e.matmul(PS[0:NST, 2048:2048 + NST], Ap_re[:], w64ct[:], start=True, stop=False),
                e.matmul(PS[0:NST, 2048:2048 + NST], Ap_im[:], w64snt[:], start=False, stop=True))[-1])
            A("V", lambda e: (
                e.tensor_copy(Ksb[:], PS[0:NST, 2048:2048 + NST]), e.drain(),
                e.tensor_tensor(Ksb[0:1, 0:1], Ksb[0:1, 0:1], sca[0:1, 1:2], ALU.add),
                e.drain())[-1])
            A("T", lambda e: e.transpose(PS[0:NST, 2048:2048 + NST], Ksb[:], idf[0:NST, 0:NST]))
            A("V", lambda e: (e.tensor_copy(KTb[:], PS[0:NST, 2048:2048 + NST]),
                              e.memset(zrow[:], 0))[-1])
            A("Y", lambda e: e.dma_start(khbm[0:KPAD0][None, :], zrow[:]))
            kst = A("Y", lambda e: e.dma_start(
                bass.AP(khbm.tensor, KPAD0, [[NST, NST], [1, NST]]), KTb[:]))
            for p in range(128):
                A("Y", (lambda p: lambda e: e.dma_start(
                    kpsk[p:p + 1, :], khbm[KPAD0 - 1023 - 8 * p:][None, 0:SKEW_W]))(p), deps=[kst])
            KPSK_DONE = len(sch.steps) - 1
            if debug:
                A("Y", lambda e: e.dma_start(dbg_kg[:], dbgk[:]), deps=[sch.last_v()])
            sch.run_block()

        with ExitStack() as c2:
            SBP = lambda name, shp, dt: c2.enter_context(nc.sbuf_tensor(name, shp, dt))
            Gf1 = SBP("Gf1", [128, 32], I32)
            Gf1f = SBP("Gf1f", [128, 32], F32)
            Gf2 = SBP("Gf2", [128, 88], I32)
            Gf2f = SBP("Gf2f", [128, 88], F32)
            growr = SBP("growr", [1, 128], F32)
            gwide = SBP("gwide", [128, 128], F32)
            OH1T = SBP("OH1T", [32, NRB1 * 128], F32)
            OH2T = SBP("OH2T", [86, NRB2 * 128], F32)
            qz1t = SBP("qz1t", [32, 2752], U32)
            zq1w = SBP("zq1w", [32, 1376], U32)
            sc1zw = SBP("sc1zw", [32, 1376], F32)
            z1fw = SBP("z1fw", [32, 1376], F32)
            qz2t = SBP("qz2t", [86, 512], U32)
            zq2w = SBP("zq2w", [86, 512], U32)
            w2zw = SBP("w2zw", [86, 512], F32)
            scw1 = SBP("scw1", [32, 2 * 512], F32)
            scw2 = SBP("scw2", [86, 2 * 512], F32)
            scst = SBP("scst", [128, 2 * 512], F32)

            lds = []
            for rb in range(NRB1):
                lds.append(A("Y", (lambda rb: lambda e: e.dma_start(
                    Gf1[:, 8 * rb:8 * rb + 8],
                    bass.AP(g1.tensor, 1024 * rb, [[8, 128], [1, 8]])))(rb), deps=[]))
            for rb2 in range(NRB2):
                rows = 128 if rb2 < 10 else 96
                lds.append(A("Y", (lambda rb2, rows: lambda e: e.dma_start(
                    Gf2[0:rows, 8 * rb2:8 * rb2 + 8],
                    bass.AP(g2.tensor, 1024 * rb2, [[8, rows], [1, 8]])))(rb2, rows), deps=[]))
            lds.append(A("Y", lambda e: e.dma_start(qz1t[:], qz1), deps=[]))
            lds.append(A("Y", lambda e: e.dma_start(qz2t[:], qz2), deps=[]))
            A("V", lambda e: (e.tensor_copy(Gf1f[:], Gf1[:]),
                              e.tensor_copy(Gf2f[:], Gf2[:]))[-1], deps=lds + base_loads)

            def _voh1(e):
                for t in range(32):
                    e.tensor_scalar(gwide[:, 0:32], io32b[:], Gf1f[:, t:t + 1], None, ALU.is_equal)
                    e.drain()
                    e.tensor_copy(OH1[:, 32 * t:32 * (t + 1)], gwide[:, 0:32])
                    e.drain()
                return e.engine_nop()
            A("V", _voh1)
            for rb in range(NRB1):
                A("T", (lambda rb: lambda e: e.transpose(
                    PS[0:8, 2048:2176], Gf1f[:, 8 * rb:8 * rb + 8], idf[:]))(rb))
                A("V", lambda e: e.tensor_copy(gwide[0:8, 0:128], PS[0:8, 2048:2176]))
                A("Y", (lambda rb: lambda e: e.dma_start(
                    grow1_hbm[rb:rb + 1, :], gwide[0:1, 0:128]))(rb))
            for rb2 in range(NRB2):
                rows = 128 if rb2 < 10 else 96
                A("T", (lambda rb2, rows: lambda e: e.transpose(
                    PS[0:8, 2048:2048 + rows], Gf2f[0:rows, 8 * rb2:8 * rb2 + 8], idf[0:rows, 0:rows]))(rb2, rows))
                A("V", (lambda rows: lambda e: e.tensor_copy(
                    gwide[0:8, 0:rows], PS[0:8, 2048:2048 + rows]))(rows))
                A("Y", (lambda rb2, rows: lambda e: e.dma_start(
                    grow2_hbm[rb2:rb2 + 1, 0:rows], gwide[0:1, 0:rows]))(rb2, rows))
            for rb in range(NRB1):
                A("Y", (lambda rb: lambda e: e.dma_start(growr[:], grow1_hbm[rb:rb + 1, :]))(rb))
                A("T", lambda e: e.matmul(PS[0:32, 2048:2176], onf[0:1, 0:32],
                                          growr[0:1, :], start=True, stop=True))
                A("V", (lambda rb: lambda e: e.tensor_scalar(
                    OH1T[:, 128 * rb:128 * (rb + 1)], PS[0:32, 2048:2176],
                    iot1[0:32, 0:1], None, ALU.is_equal))(rb))
            for rb2 in range(NRB2):
                cols = 128 if rb2 < 10 else 96
                A("Y", (lambda rb2, cols: lambda e: e.dma_start(
                    growr[0:1, 0:cols], grow2_hbm[rb2:rb2 + 1, 0:cols]))(rb2, cols))
                A("T", (lambda cols: lambda e: e.matmul(
                    PS[0:86, 2048:2048 + cols], onf[0:1, 0:86],
                    growr[0:1, 0:cols], start=True, stop=True))(cols))
                A("V", (lambda rb2, cols: lambda e: e.tensor_scalar(
                    OH2T[:, 128 * rb2:128 * rb2 + cols], PS[0:86, 2048:2048 + cols],
                    iot1[0:86, 0:1], None, ALU.is_equal))(rb2, cols))
            OH_DONE = len(sch.steps) - 1

            for w in range(16):
                lw = A("Y", (lambda w: lambda e: e.dma_start(
                    sc1zw[:], sc1[:, 1376 * w:1376 * (w + 1)]))(w))

                def _vz1(e, w=w):
                    for k in range(8):
                        e.tensor_scalar(bass.AP(zq1w, k, [[1376, 32], [8, 172]]),
                                        qz1t[:, 172 * w:172 * (w + 1)],
                                        4 * k, 15, ALU.logical_shift_right, ALU.bitwise_and)
                    e.drain()
                    e.tensor_scalar(zq1w[:], zq1w[:], 1, None, ALU.add)
                    e.drain()
                    e.tensor_tensor(z1fw[:], zq1w[:], sc1zw[:], ALU.mult)
                    e.drain()
                    e.tensor_scalar(z1fw[:], z1fw[:], -1.0, None, ALU.mult)
                    e.drain()
                    return e.tensor_copy(Z1b[:, 1376 * w:1376 * (w + 1)], z1fw[:])
                A("V", _vz1, deps=[lw])

            cp_hist, dm_hist, mm_hist = [], [], []

            def expjob(mmfn, rows, dst, extra_mm_deps):
                i = len(mm_hist)
                bk = i % 2
                mm = A("T", (lambda mmfn, rows, bk: lambda e: mmfn(e, BANK(bk)[0:rows, :]))(
                    mmfn, rows, bk),
                    deps=extra_mm_deps + [cp_hist[-2] if len(cp_hist) >= 2 else None])
                cp = A("S", (lambda rows, bk: lambda e: e.copy(
                    scst[0:rows, 512 * bk:512 * (bk + 1)], BANK(bk)[0:rows, :]))(rows, bk),
                    deps=[mm, dm_hist[-2] if len(dm_hist) >= 2 else None])
                dmst = A("SD", (lambda rows, bk, dst: lambda e: e.dma_start(
                    dst, scst[0:rows, 512 * bk:512 * (bk + 1)]))(rows, bk, dst),
                    deps=[cp])
                mm_hist.append(mm)
                cp_hist.append(cp)
                dm_hist.append(dmst)
                return mm

            for ch in range(GU // 512):
                sl = ch % 2
                dep_slot = mm_hist[-(2 * NRB1 - NRB1 + 1)] if len(mm_hist) >= NRB1 + 1 else None
                lw = A("Y", (lambda ch, sl: lambda e: e.dma_start(
                    scw1[:, 512 * sl:512 * (sl + 1)], sc1[:, 512 * ch:512 * (ch + 1)]))(ch, sl),
                    deps=[mm_hist[-NRB1] if len(mm_hist) >= NRB1 else None])
                for rb in range(NRB1):
                    expjob((lambda rb, sl: lambda e, out: e.matmul(
                        out, OH1T[0:32, 128 * rb:128 * (rb + 1)],
                        scw1[0:32, 512 * sl:512 * (sl + 1)], start=True, stop=True))(rb, sl),
                        128,
                        bass.AP(sc1e_s.tensor, (128 * rb) * GU + 512 * ch, [[GU, 128], [1, 512]]),
                        [lw, OH_DONE])
            for ch in range(DM // 512):
                sl = ch % 2
                lw = A("Y", (lambda ch, sl: lambda e: e.dma_start(
                    scw2[:, 512 * sl:512 * (sl + 1)], sc2[:, 512 * ch:512 * (ch + 1)]))(ch, sl),
                    deps=[mm_hist[-2 * NRB2] if len(mm_hist) >= 2 * NRB2 else None])

                def _vw2z(e, ch=ch, sl=sl):
                    for k in range(8):
                        e.tensor_scalar(bass.AP(zq2w, k, [[512, 86], [8, 64]]),
                                        qz2t[:, 64 * ch:64 * (ch + 1)],
                                        4 * k, 15, ALU.logical_shift_right, ALU.bitwise_and)
                    e.drain()
                    e.tensor_scalar(zq2w[:], zq2w[:], 1, None, ALU.add)
                    e.drain()
                    return e.tensor_tensor(w2zw[:], zq2w[:],
                                           scw2[0:86, 512 * sl:512 * (sl + 1)], ALU.mult)
                vz = A("V", _vw2z, deps=[lw, mm_hist[-1] if mm_hist else None])
                for rb2 in range(NRB2):
                    rows = 128 if rb2 < 10 else 96
                    expjob((lambda rb2, sl, rows: lambda e, out: e.matmul(
                        out, OH2T[0:86, 128 * rb2:128 * rb2 + rows],
                        scw2[0:86, 512 * sl:512 * (sl + 1)], start=True, stop=True))(rb2, sl, rows),
                        rows,
                        bass.AP(sc2e_s.tensor, (128 * rb2) * DM + 512 * ch, [[DM, rows], [1, 512]]),
                        [lw, OH_DONE])
                    expjob((lambda rb2, rows: lambda e, out: e.matmul(
                        out, OH2T[0:86, 128 * rb2:128 * rb2 + rows],
                        w2zw[0:86, :], start=True, stop=True))(rb2, rows),
                        rows,
                        bass.AP(z2e_s.tensor, (128 * rb2) * DM + 512 * ch, [[DM, rows], [1, 512]]),
                        [vz, OH_DONE])
            SCE_ALL = dm_hist[-1]
            sch.run_block()

        with ExitStack() as c3:
            SBP = lambda name, shp, dt: c3.enter_context(nc.sbuf_tensor(name, shp, dt))
            xtl = SBP("xtl", [128, 2 * DM], F32)
            xnf = SBP("xnf", [128, DM], F32)
            st6 = SBP("st6", [128, 48], F32)
            dbgl = SBP("dbgl", [128, 64], F32) if debug else None
            ag2 = SBP("ag2", [128, 2], F32)
            lnst = SBP("lnst", [128, 8], F32)

            xload, xnsteps, trsteps = {}, {}, {}
            for tt in range(NTT):
                sl = tt % 2
                xt = xtl[:, sl * DM:(sl + 1) * DM]
                xload[tt] = A("Y", (lambda tt, xt: lambda e: e.dma_start(
                    xt, x_in[128 * tt:128 * (tt + 1), :]))(tt, xt),
                    deps=[] if tt < 2 else [xnsteps[tt - 2]])

                def _vln(e, xt=xt):
                    for cc in range(8):
                        e.bn_stats(st6[:, 6 * cc:6 * cc + 6], xt[:, 512 * cc:512 * (cc + 1)])
                    e.drain()
                    e.bn_aggr(ag2[:], st6[:])
                    e.drain()
                    e.tensor_scalar(lnst[:, 0:1], ag2[:, 1:2], EPS, None, ALU.add)
                    return e.drain()
                v1 = A("V", _vln, deps=[xload[tt]])
                s1 = A("S", lambda e: e.activation(lnst[:, 1:2], lnst[:, 0:1], AF.Sqrt), deps=[v1])

                def _vxn(e, xt=xt):
                    e.reciprocal(lnst[:, 2:3], lnst[:, 1:2])
                    e.drain()
                    e.tensor_tensor(lnst[:, 3:4], ag2[:, 0:1], lnst[:, 2:3], ALU.mult)
                    e.drain()
                    e.tensor_scalar(lnst[:, 4:5], lnst[:, 3:4], -1.0, None, ALU.mult)
                    e.drain()
                    return e.tensor_scalar(xnf[:], xt, lnst[:, 2:3], lnst[:, 4:5], ALU.mult, ALU.add)
                xnsteps[tt] = A("V", _vxn,
                                deps=[s1, trsteps.get((tt - 1, 7))])
                if debug:
                    A("V", (lambda tt: lambda e: (
                        e.tensor_copy(dbgl[:, 8 * tt:8 * tt + 1], ag2[:, 0:1]),
                        e.tensor_copy(dbgl[:, 8 * tt + 1:8 * tt + 2], ag2[:, 1:2]),
                        e.tensor_copy(dbgl[:, 8 * tt + 2:8 * tt + 5], lnst[:, 0:3]))[-1])(tt))
                for grp in range(8):
                    bk = 2 + (grp % 2)

                    def _ttr(e, grp=grp, bk=bk):
                        lastv = None
                        for j in range(4):
                            t = 4 * grp + j
                            rb, k = t // 8, t % 8
                            src = bass.AP(xnf, 1024 * rb + k, [[DM, 128], [8, 128]])
                            lastv = e.transpose(PS[:, 512 * bk + 128 * j:512 * bk + 128 * (j + 1)],
                                                src, idf[:])
                        return lastv
                    prev_ev = trsteps.get((tt, grp - 2)) or trsteps.get((tt - 1, grp + 6))
                    t1 = A("T", _ttr, deps=[xnsteps[tt], prev_ev])

                    def _sev(e, tt=tt, grp=grp, bk=bk):
                        dstap = bass.AP(xnT, TOK * (4 * grp) + 128 * tt,
                                        [[32 * TOK, 128], [TOK, 4], [1, 128]])
                        return e.copy(dstap, PS[:, 512 * bk:512 * (bk + 1)])
                    trsteps[(tt, grp)] = A("S", _sev, deps=[t1])
            LN_ALL = trsteps[(NTT - 1, 7)]

            def _ts1(e):
                lastv = None
                for t in range(32):
                    lastv = e.matmul(PS[0:32, 3072:3584], OH1[:, 32 * t:32 * (t + 1)],
                                     xnT[:, TOK * t:TOK * t + 512],
                                     start=(t == 0), stop=(t == 31))
                for t in range(32):
                    lastv = e.matmul(PS[0:32, 3584:4096], OH1[:, 32 * t:32 * (t + 1)],
                                     xnT[:, TOK * t + 512:TOK * t + 1024],
                                     start=(t == 0), stop=(t == 31))
                return lastv
            ts1 = A("T", _ts1, deps=[LN_ALL])
            vs1 = A("V", lambda e: (e.tensor_copy(S1b[:, 0:512], PS[0:32, 3072:3584]),
                                    e.tensor_copy(S1b[:, 512:1024], PS[0:32, 3584:4096]))[-1],
                    deps=[ts1])
            if debug:
                A("Y", lambda e: e.dma_start(dbg_ln[:], dbgl[:]), deps=[vs1])
            sch.run_block()

        conv_ev, conv_st = {}, {}
        for tt in range(NTT):
            def _tconv(e, tt=tt):
                lastv = None
                for rb in range(4):
                    for k in range(8):
                        t = 8 * rb + k
                        lh = xnT[:, TOK * t + 128 * tt:TOK * t + 128 * (tt + 1)]
                        for b in range(2 * rb, 8):
                            m0 = 1023 + 512 * b - 1024 * rb - k
                            lastv = e.matmul(BANK(b), lh, kpsk[:, m0:m0 + 512],
                                             start=(rb == 0 and k == 0),
                                             stop=(rb == b // 2 and k == 7))
                return lastv
            deps = [KPSK_DONE]
            if tt > 0:
                deps += [conv_ev[(tt - 1, b)] for b in range(8)]
            tmm = A("T", _tconv, deps=deps)
            for b in range(8):
                conv_ev[(tt, b)] = A("S", (lambda b: lambda e: e.copy(
                    esb[:, 512 * b:512 * (b + 1)], BANK(b)))(b),
                    deps=[tmm, conv_st.get((tt - 1, b))])
                conv_st[(tt, b)] = A("SD", (lambda tt, b: lambda e: e.dma_start(
                    s_s[128 * tt:128 * (tt + 1), 512 * b:512 * (b + 1)],
                    esb[:, 512 * b:512 * (b + 1)]))(tt, b),
                    deps=[conv_ev[(tt, b)]])
        CONV_ALL = conv_st[(NTT - 1, 7)]
        sch.run_block()

        with ExitStack() as c5:
            SBP = lambda name, shp, dt: c5.enter_context(nc.sbuf_tensor(name, shp, dt))
            qwt = SBP("qwt", [128, 2 * 4 * 256], U32)
            scet = SBP("scet", [128, 2 * 4 * 256], F32)
            p16 = SBP("p16", [128, 256], U16)
            Wt = SBP("Wt", [128, 2 * 32 * 256], BF16)
            sgb = SBP("sgb", [128, 2 * 1024], BF16)
            hst = SBP("hst", [128, 2 * 1024], BF16)

            m1_ld, m1_dq, m1_mm, m1_sil, m1_hm, m1_hd = {}, {}, {}, {}, {}, {}
            for mo in range(NPAIR + 1):
                m = mo
                if m < NPAIR:
                    sl = m % 2
                    lds = []
                    for rb in range(NRB1):
                        dep = [SCE_ALL] if m < 2 else [m1_dq[m - 2]]
                        lds.append(A("Y", (lambda m, rb, sl: lambda e: e.dma_start(
                            bass.AP(qwt, (sl * 4 + rb) * 256, [[2048, 128], [1, 256]]),
                            bass.AP(qw1.tensor, (128 * rb) * GU + 128 * m,
                                    [[GU, 128], [INTER, 2], [1, 128]])))(m, rb, sl), deps=dep))
                        lds.append(A("Y", (lambda m, rb, sl: lambda e: e.dma_start(
                            bass.AP(scet, (sl * 4 + rb) * 256, [[2048, 128], [1, 256]]),
                            bass.AP(sc1e_s.tensor, (128 * rb) * GU + 128 * m,
                                    [[GU, 128], [INTER, 2], [1, 128]])))(m, rb, sl), deps=dep))
                    m1_ld[m] = lds

                    def _vdq(e, sl=sl):
                        lastv = None
                        u16v = qwt[:].bitcast(U16)
                        for rb in range(NRB1):
                            base = (sl * 4 + rb) * 256
                            for k in range(8):
                                h16, mm_ = k // 4, k % 4
                                e.tensor_scalar(p16[:],
                                                u16v[:, 2 * base + h16:2 * base + 512:2],
                                                4 * mm_, 15,
                                                ALU.logical_shift_right, ALU.bitwise_and)
                                t = 8 * rb + k
                                lastv = e.tensor_tensor(
                                    Wt[:, (sl * 32 + t) * 256:(sl * 32 + t) * 256 + 256],
                                    p16[:], scet[:, base:base + 256], ALU.mult)
                        return lastv
                    m1_dq[m] = A("V", _vdq,
                                 deps=m1_ld[m] + ([m1_mm[m - 2]] if m >= 2 else []))

                    def _tmm(e, m=m, sl=sl):
                        P4 = 4 * sl
                        for rb in range(NRB1):
                            for k in range(8):
                                t = 8 * rb + k
                                wb = (sl * 32 + t) * 256
                                st_ = (rb == 0 and k == 0)
                                e.matmul(BANK(P4 + 0), Wt[:, wb:wb + 128],
                                         xnT[:, TOK * t:TOK * t + 512], start=st_, stop=False)
                                e.matmul(BANK(P4 + 1), Wt[:, wb:wb + 128],
                                         xnT[:, TOK * t + 512:TOK * t + 1024], start=st_, stop=False)
                                e.matmul(BANK(P4 + 2), Wt[:, wb + 128:wb + 256],
                                         xnT[:, TOK * t:TOK * t + 512], start=st_, stop=False)
                                e.matmul(BANK(P4 + 3), Wt[:, wb + 128:wb + 256],
                                         xnT[:, TOK * t + 512:TOK * t + 1024], start=st_, stop=False)
                        e.matmul(BANK(P4 + 0), Z1b[0:32, 128 * m:128 * m + 128],
                                 S1b[0:32, 0:512], start=False, stop=True)
                        e.matmul(BANK(P4 + 1), Z1b[0:32, 128 * m:128 * m + 128],
                                 S1b[0:32, 512:1024], start=False, stop=True)
                        e.matmul(BANK(P4 + 2), Z1b[0:32, INTER + 128 * m:INTER + 128 * m + 128],
                                 S1b[0:32, 0:512], start=False, stop=True)
                        return e.matmul(BANK(P4 + 3), Z1b[0:32, INTER + 128 * m:INTER + 128 * m + 128],
                                        S1b[0:32, 512:1024], start=False, stop=True)
                    mmdeps = [m1_dq[m]]
                    if m == 0:
                        mmdeps.append(CONV_ALL)
                    if m >= 2:
                        mmdeps += [m1_sil[m - 2], m1_hm[m - 2]]
                    m1_mm[m] = A("T", _tmm, deps=mmdeps)
                    m1_sil[m] = A("S", (lambda sl: lambda e: (
                        e.activation(sgb[:, sl * 1024:sl * 1024 + 512], BANK(4 * sl + 0), AF.Silu),
                        e.activation(sgb[:, sl * 1024 + 512:sl * 1024 + 1024],
                                     BANK(4 * sl + 1), AF.Silu))[-1])(sl),
                        deps=[m1_mm[m]] + ([m1_hm[m - 2]] if m >= 2 else []))
                m1_ = mo - 1
                if 0 <= m1_ < NPAIR:
                    sl1 = m1_ % 2
                    m1_hm[m1_] = A("V", (lambda sl: lambda e: (
                        e.tensor_tensor(hst[:, sl * 1024:sl * 1024 + 512],
                                        sgb[:, sl * 1024:sl * 1024 + 512],
                                        BANK(4 * sl + 2), ALU.mult),
                        e.tensor_tensor(hst[:, sl * 1024 + 512:sl * 1024 + 1024],
                                        sgb[:, sl * 1024 + 512:sl * 1024 + 1024],
                                        BANK(4 * sl + 3), ALU.mult))[-1])(sl1),
                        deps=[m1_sil[m1_]] + ([m1_hd[m1_ - 2]] if m1_ >= 2 else []))
                    m1_hd[m1_] = A("SD", (lambda m1_, sl: lambda e: e.dma_start(
                        hact_s[128 * m1_:128 * (m1_ + 1), :],
                        hst[:, sl * 1024:sl * 1024 + 1024]))(m1_, sl1),
                        deps=[m1_hm[m1_]])
            HACT_ALL = m1_hd[NPAIR - 1]
            sch.run_block()

        with ExitStack() as c6:
            SBP = lambda name, shp, dt: c6.enter_context(nc.sbuf_tensor(name, shp, dt))
            qb2 = SBP("qb2", [128, 2 * 512], U32)
            sce2 = SBP("sce2", [128, 2 * 512], F32)
            z2t = SBP("z2t", [128, 2 * 512], F32)
            p16b = SBP("p16b", [128, 512], U16)
            Wm2 = SBP("Wm2", [128, 512], F32)
            W2t = SBP("W2t", [128, 2 * 8 * 512], BF16)
            hbuf = SBP("hbuf", [128, 4 * TOK], BF16)
            gst = SBP("gst", [128, 2 * 4 * 1024], BF16)

            m2_dq, m2_t, m2_ev, m2_gd, m2_h = {}, {}, {}, {}, {}
            hring_user = {}
            for hg in range(8):
                for rb2 in range(NRB2):
                    rows = 128 if rb2 < 10 else 96
                    sl = rb2 % 2
                    i = hg * NRB2 + rb2
                    dep = [SCE_ALL, HACT_ALL] if i < 2 else [m2_dq[i - 2]]
                    l1 = A("Y", (lambda hg, rb2, rows, sl: lambda e: e.dma_start(
                        qb2[0:rows, 512 * sl:512 * (sl + 1)],
                        qw2[128 * rb2:128 * rb2 + rows, 512 * hg:512 * (hg + 1)]))(hg, rb2, rows, sl),
                        deps=dep)
                    l2 = A("Y", (lambda hg, rb2, rows, sl: lambda e: e.dma_start(
                        sce2[0:rows, 512 * sl:512 * (sl + 1)],
                        bass.AP(sc2e_s.tensor, (128 * rb2) * DM + 512 * hg,
                                [[DM, rows], [1, 512]])))(hg, rb2, rows, sl), deps=dep)
                    l3 = A("Y", (lambda hg, rb2, rows, sl: lambda e: e.dma_start(
                        z2t[0:rows, 512 * sl:512 * (sl + 1)],
                        bass.AP(z2e_s.tensor, (128 * rb2) * DM + 512 * hg,
                                [[DM, rows], [1, 512]])))(hg, rb2, rows, sl), deps=dep)

                    def _vdq2(e, rows=rows, sl=sl):
                        lastv = None
                        u16v = qb2[:].bitcast(U16)
                        base = 512 * sl
                        for k2 in range(8):
                            h16, mm_ = k2 // 4, k2 % 4
                            e.tensor_scalar(p16b[0:rows, :],
                                            u16v[0:rows, 2 * base + h16:2 * base + 1024:2],
                                            4 * mm_, 15, ALU.logical_shift_right, ALU.bitwise_and)
                            e.tensor_tensor(Wm2[0:rows, :], p16b[0:rows, :],
                                            sce2[0:rows, base:base + 512], ALU.mult)
                            lastv = e.tensor_tensor(
                                W2t[0:rows, (sl * 8 + k2) * 512:(sl * 8 + k2 + 1) * 512],
                                Wm2[0:rows, :], z2t[0:rows, base:base + 512], ALU.subtract)
                        return lastv
                    wprev = (m2_t[(hg, rb2 - 2, 7)] if rb2 >= 2 else
                             (m2_t[(hg - 1, NRB2 - 2 + rb2, 7)] if hg >= 1 else None))
                    m2_dq[i] = A("V", _vdq2, deps=[l1, l2, l3, wprev])
                    for k2 in range(8):
                        hidx = i * 8 + k2
                        ring = hidx % 4
                        m2_h[hidx] = A("Y", (lambda rb2, k2, rows, ring: lambda e: e.dma_start(
                            hbuf[0:rows, TOK * ring:TOK * (ring + 1)],
                            bass.AP(hact_s.tensor, (8 * 128 * rb2 + k2) * TOK,
                                    [[8 * TOK, rows], [1, TOK]])))(rb2, k2, rows, ring),
                            deps=[HACT_ALL, hring_user.get(hidx - 4)])

                        def _tm2(e, rb2=rb2, k2=k2, rows=rows, sl=sl, ring=ring):
                            lastv = None
                            st_ = (rb2 == 0 and k2 == 0)
                            sp_ = (rb2 == NRB2 - 1 and k2 == 7)
                            for ht in range(4):
                                wb = (sl * 8 + k2) * 512 + 128 * ht
                                lastv = e.matmul(BANK(2 * ht), W2t[0:rows, wb:wb + 128],
                                                 hbuf[0:rows, TOK * ring:TOK * ring + 512],
                                                 start=st_, stop=sp_)
                                lastv = e.matmul(BANK(2 * ht + 1), W2t[0:rows, wb:wb + 128],
                                                 hbuf[0:rows, TOK * ring + 512:TOK * ring + 1024],
                                                 start=st_, stop=sp_)
                            return lastv
                        tdeps = [m2_dq[i], m2_h[hidx]]
                        if rb2 == 0 and k2 == 0:
                            if hg > 0:
                                tdeps += [m2_ev[(hg - 1, bb)] for bb in range(8)]
                            else:
                                tdeps.append(CONV_ALL)
                        m2_t[(hg, rb2, k2)] = A("T", _tm2, deps=tdeps)
                        hring_user[hidx] = m2_t[(hg, rb2, k2)]
                gsl = hg % 2
                for ht in range(4):
                    for half in range(2):
                        m2_ev[(hg, 2 * ht + half)] = A("S", (lambda hg, ht, half, gsl: lambda e:
                            e.activation(gst[:, gsl * 4096 + ht * 1024 + 512 * half:
                                             gsl * 4096 + ht * 1024 + 512 * (half + 1)],
                                         BANK(2 * ht + half), AF.Silu))(hg, ht, half, gsl),
                            deps=[m2_t[(hg, NRB2 - 1, 7)], m2_gd.get((hg - 2, ht))])
                for ht in range(4):
                    m2_gd[(hg, ht)] = A("SD", (lambda hg, ht, gsl: lambda e: e.dma_start(
                        g_s[512 * hg + 128 * ht:512 * hg + 128 * (ht + 1), :],
                        gst[:, gsl * 4096 + ht * 1024:gsl * 4096 + (ht + 1) * 1024]))(hg, ht, gsl),
                        deps=[m2_ev[(hg, 2 * ht)], m2_ev[(hg, 2 * ht + 1)]])
            G_ALL = m2_gd[(7, 3)]
            sch.run_block()

        with ExitStack() as c7:
            SBP = lambda name, shp, dt: c7.enter_context(nc.sbuf_tensor(name, shp, dt))
            exb = SBP("exb", [128, DM], F32)
            egb = SBP("egb", [128, 2 * DM], BF16)
            etb = SBP("etb", [128, DM], F32)
            eob = SBP("eob", [128, DM], F32)

            vprev, oprev = None, None
            for tt in range(NTT):
                sl = tt % 2
                xl = A("Y", (lambda tt: lambda e: e.dma_start(
                    exb[:], x_in[128 * tt:128 * (tt + 1), :]))(tt),
                    deps=[vprev, CONV_ALL])
                slo = A("Y", (lambda tt: lambda e: e.dma_start(
                    esb[:], s_s[128 * tt:128 * (tt + 1), :]))(tt),
                    deps=[vprev, CONV_ALL])
                gl = A("Y", (lambda tt, sl: lambda e: e.dma_start_transpose(
                    egb[:, sl * DM:(sl + 1) * DM], g_s[:, 128 * tt:128 * (tt + 1)]))(tt, sl),
                    deps=[G_ALL] if tt < 2 else [G_ALL, vprev])
                vstep = A("V", (lambda sl: lambda e: (
                    e.tensor_tensor(etb[:], esb[:], egb[:, sl * DM:(sl + 1) * DM], ALU.mult),
                    e.tensor_tensor(eob[:], etb[:], exb[:], ALU.add))[-1])(sl),
                    deps=[xl, slo, gl, oprev])
                oprev = A("SD", (lambda tt: lambda e: e.dma_start(
                    y_out[128 * tt:128 * (tt + 1), :], eob[:]))(tt), deps=[vstep])
                vprev = vstep
            if debug:
                A("Y", lambda e: e.dma_start(dbg_xnT[:], xnT[:]), deps=[oprev])
                A("Y", lambda e: e.dma_start(dbg_S1b[:], S1b[:]))
                A("Y", lambda e: e.dma_start(dbg_Z1b[:], Z1b[:]))
            A("V", lambda e: e.engine_nop(), deps=[len(sch.steps) - 1])
            sch.run_block()

    return nc


def kernel(**inputs):
    u = np.ascontiguousarray(inputs["L0_attn_u"], dtype=np.float32).reshape(8192, DM)
    g1v = np.asarray(inputs["L0_ffn_g_idx_gu"])
    g2v = np.asarray(inputs["L0_ffn_g_idx_down"])
    assert np.array_equal(g1v, np.arange(DM, dtype=np.int64) // 128), "unsupported g_idx_gu"
    assert np.array_equal(g2v, np.arange(INTER, dtype=np.int64) // 128), "unsupported g_idx_down"
    consts = _host_constants()
    common = dict(
        lam=np.ascontiguousarray(inputs["L0_attn_Lambda_ri"], dtype=np.float32),
        pv=np.ascontiguousarray(inputs["L0_attn_p_ri"], dtype=np.float32),
        qv=np.ascontiguousarray(inputs["L0_attn_q_ri"], dtype=np.float32),
        vc=np.ascontiguousarray(inputs["L0_attn_Vc_ri"], dtype=np.float32).reshape(NST, 2 * NST),
        ct=np.ascontiguousarray(inputs["L0_attn_Ct_ri"], dtype=np.float32).reshape(NST, 2),
        bp=np.ascontiguousarray(inputs["L0_attn_Bp"], dtype=np.float32).reshape(NST, 1),
        dd=np.ascontiguousarray(inputs["L0_attn_D"], dtype=np.float32).reshape(1, 1),
        ls=np.ascontiguousarray(inputs["L0_attn_log_step"], dtype=np.float32).reshape(1, 1),
        qw1=np.ascontiguousarray(inputs["L0_ffn_qweight_gu"]),
        qz1=np.ascontiguousarray(inputs["L0_ffn_qzeros_gu"]),
        sc1=np.ascontiguousarray(inputs["L0_ffn_scales_gu"], dtype=np.float32),
        g1=np.ascontiguousarray(g1v, dtype=np.int32),
        qw2=np.ascontiguousarray(inputs["L0_ffn_qweight_down"]),
        qz2=np.ascontiguousarray(inputs["L0_ffn_qzeros_down"]),
        sc2=np.ascontiguousarray(inputs["L0_ffn_scales_down"], dtype=np.float32),
        g2=np.ascontiguousarray(g2v, dtype=np.int32),
        **consts,
    )
    nc = build_program(debug=DEBUG)
    in_maps = []
    for c in range(NCORES):
        m = dict(common)
        m["x"] = np.ascontiguousarray(u[TOK * c:TOK * (c + 1)])
        in_maps.append(m)
    res = run_bass_kernel_spmd(nc, in_maps, core_ids=list(range(NCORES)))
    y = np.concatenate([np.asarray(r["y"], dtype=np.float32) for r in res.results], axis=0)
    if DEBUG:
        global DEBUG_RES
        DEBUG_RES = res.results
    return y.reshape(4, 2048, DM)
